# version 02 (final)
# baseline (speedup 1.0000x reference)
import math
import numpy as np
import ml_dtypes
import concourse.bass as bass
import concourse.mybir as mybir
from concourse.bass_utils import run_bass_kernel_spmd

F32 = mybir.dt.float32
BF16 = mybir.dt.bfloat16
I32 = mybir.dt.int32
I16 = mybir.dt.int16
U8 = mybir.dt.uint8
AF = mybir.ActivationFunctionType
ALU = mybir.AluOpType
AX = mybir.AxisListType
DS = bass.ds

NCORE = 8
SEQ = 2048
DM = 2048
TOWN = 1024
NCHUNK = 4
SOFT_C = 24.0
NEG_BIG = -1e30
SLOTS = [2, 4]
NSLOT = sum(SLOTS)
LN2_E = float(np.log(10000.0))
L0_NH, L0_NKV, L0_HD = 32, 4, 64
L1_NH, L1_HD = 16, 128
NW = 8192 * 2048

FP32_EPS_LN = 1e-5
FP32_EPS_BL = 1e-6
CLIP_HI = float(np.float32(128.0) - np.float32(1e-6))


def chunk_assign(core):
    seq = core // 2
    return ([0, 3] if core % 2 == 0 else [1, 2]), seq


def slot_meta(core):
    qbs, _ = chunk_assign(core)
    idx, vb = [], []
    for qi, qc in enumerate(qbs):
        lst = [qc] + [c for c in range(NCHUNK) if c < qc]
        for j in range(SLOTS[qi]):
            if j < len(lst):
                idx.append(lst[j]); vb.append(-SOFT_C)
            else:
                idx.append(0); vb.append(NEG_BIG)
    while len(idx) < 8:
        idx.append(0); vb.append(NEG_BIG)
    return np.array(idx, dtype=np.int32), np.array(vb, dtype=np.float32)


def slot_meta_full(core):
    qbs, _ = chunk_assign(core)
    idx = [qbs[0], qbs[1], 0, 1, 2, 3, 0, 0]
    vb = [-SOFT_C] * 8
    return np.array(idx, dtype=np.int32), np.array(vb, dtype=np.float32)



class Step:
    __slots__ = ("engine", "emit", "inc_n", "sem_val", "waits")


class Plan:
    ENGINES = ("sync", "scalar", "vector", "tensor", "gpsimd")

    def __init__(self, nc):
        self.nc = nc
        self.steps = []
        self.sem_count = {e: 0 for e in self.ENGINES}
        self.last_write = {}
        self.readers = {}
        self.bar = {}

    def add(self, engine, emit, reads=(), writes=(), dma=False):
        st = Step()
        st.engine = engine
        st.emit = emit
        st.inc_n = 16 if dma else 1
        waits = dict(self.bar)

        def need(dep):
            e, v, isdma = dep
            if isdma:
                v = self.sem_count[e]
            if waits.get(e, 0) < v:
                waits[e] = v
        if Plan.SERIALIZE or getattr(self, "ser_on", False):
            for e, v in self.sem_count.items():
                if v and waits.get(e, 0) < v:
                    waits[e] = v
        for r in reads:
            lw = self.last_write.get(r)
            if lw:
                need(lw)
        for w in writes:
            lw = self.last_write.get(w)
            if lw:
                need(lw)
            for rd in self.readers.get(w, ()):
                need(rd)
        self.sem_count[engine] += st.inc_n
        st.sem_val = self.sem_count[engine]
        st.waits = sorted(waits.items())
        self.steps.append(st)
        dep = (engine, st.sem_val, dma)
        for w in writes:
            self.last_write[w] = dep
            self.readers[w] = []
        for r in reads:
            self.readers.setdefault(r, []).append(dep)
        return st

    def barrier(self):
        self.bar = dict(self.sem_count)

    SERIALIZE = False

    def emit_all(self, block, sems):
        steps = self.steps

        def run_engine(eng_obj, ename):
            for st in steps:
                if st.engine != ename:
                    continue
                for (we, wv) in st.waits:
                    eng_obj.wait_ge(sems[we], wv)
                st.emit(eng_obj).then_inc(sems[st.engine], st.inc_n)

        @block.sync
        def _(sync):
            run_engine(sync, "sync")

        @block.scalar
        def _(scalar):
            run_engine(scalar, "scalar")

        @block.vector
        def _(vector):
            run_engine(vector, "vector")

        @block.tensor
        def _(tensor):
            run_engine(tensor, "tensor")

        @block.gpsimd
        def _(gpsimd):
            gpsimd.enable_hardware_checks = False
            run_engine(gpsimd, "gpsimd")



def e_dma(out_ap, in_ap, **kw):
    return lambda eng: eng.dma_start(out=out_ap, in_=in_ap, **kw)


def e_memset(ap, val):
    return lambda eng: eng.memset(ap, val)


def e_ts(out, in0, s1, op0, s2=None, op1=None):
    if op1 is None:
        return lambda eng: eng.tensor_scalar(out, in0, s1, None, op0)
    return lambda eng: eng.tensor_scalar(out, in0, s1, s2, op0, op1)


def e_stt(out, in0, scalar, in1, op0, op1):
    return lambda eng: eng.scalar_tensor_tensor(out, in0, scalar, in1, op0, op1)


def e_tt(out, in0, in1, op):
    return lambda eng: eng.tensor_tensor(out, in0, in1, op)


def e_copy(out, in_):
    def f(eng):
        if hasattr(eng, "tensor_copy"):
            return eng.tensor_copy(out, in_)
        return eng.copy(out, in_)
    return f


def e_act(out, in_, func, bias=0.0, scale=1.0, accum_out=None):
    return lambda eng: eng.activation(out, in_, func, bias=bias, scale=scale,
                                      accum_out=accum_out)


def e_reduce(out, in_, op, absval=False):
    return lambda eng: eng.tensor_reduce(out, in_, AX.X, op,
                                         apply_absolute_value=absval)


def e_recip(out, in_):
    return lambda eng: eng.reciprocal(out, in_)


def e_mm(out, lhsT, rhs, start=True, stop=True):
    return lambda eng: eng.matmul(out, lhsT, rhs, start=start, stop=stop)


DRAIN_MM = {"ap": None}


def e_mm_chain(chain):
    def f(eng):
        inst = None
        for (out, lhsT, rhs, st, sp) in chain:
            inst = eng.matmul(out, lhsT, rhs, start=st, stop=sp)
        if DRAIN_MM["ap"] is not None:
            (out, lhsT, rhs, st, sp) = chain[-1]
            inst = eng.matmul(DRAIN_MM["ap"], lhsT[:, 0:1], rhs[:, 0:1],
                              start=True, stop=True)
        return inst
    return f


def e_transpose(out, in_, ident):
    return lambda eng: eng.transpose(out, in_, ident)


def e_iota(out, pattern, base=0, cm=0):
    return lambda eng: eng.iota(out, pattern=pattern, base=base,
                                channel_multiplier=cm,
                                allow_small_or_imprecise_dtypes=True)


def e_affine_sel(out, in_, cmp, fill, base, pattern, cm):
    return lambda eng: eng.affine_select(out=out, in_=in_, compare_op=cmp,
                                         fill=fill, base=base, pattern=pattern,
                                         channel_multiplier=cm)


def e_init_snaps(b, sidx, n, mult, store):
    def f(eng):
        d = {}
        setattr(b, store, d)
        for si in range(n):
            reg = eng.alloc_register()
            eng.reg_load(reg, sidx[0:1, si:si + 1])
            eng.reg_alu(reg, reg, mult, ALU.mult)
            d[si] = eng.snap(reg, donate=True, min_val=0,
                             max_val=(NCHUNK - 1) * mult)
        return eng.nop()
    return f


def e_ds2(b, out_ap, src_fn, si, store="snapvals"):
    return lambda eng: eng.dma_start(out=out_ap,
                                     in_=src_fn(getattr(b, store)[si]))


def e_nop():
    return lambda eng: eng.engine_nop()



class Builder:
    def __init__(self, nc):
        self.nc = nc
        self.p = Plan(nc)
        self._stack = [[]]

    DBG_ALLOC = False

    def sbuf(self, name, shape, dtype):
        h = self.nc.sbuf_tensor(name, shape, dtype)
        t = h.__enter__()
        if Builder.DBG_ALLOC:
            import numpy as _np
            nb = int(_np.prod(shape[1:])) * mybir.dt.size(dtype)
            self._dbg_total = getattr(self, "_dbg_total", 0) + nb
            print(f"[alloc] {name} {shape} {dtype} {nb}B total~{self._dbg_total}")
        self._stack[-1].append(h)
        return t

    def psum(self, name, shape, dtype=F32):
        h = self.nc.psum_tensor(name, shape, dtype)
        t = h.__enter__()
        self._stack[-1].append(h)
        return t

    def push(self):
        self._stack.append([])

    def pop(self):
        for h in reversed(self._stack.pop()):
            h.__exit__(None, None, None)
        self.p.barrier()

    def close(self):
        while self._stack:
            for h in reversed(self._stack.pop()):
                h.__exit__(None, None, None)


    def constants(self):
        b, p = self, self.p
        self.ident = b.sbuf("ident", [128, 128], BF16)
        p.add("vector", e_memset(self.ident[:], 1.0), writes=["identf"])
        p.add("gpsimd", e_affine_sel(self.ident[:], self.ident[:], ALU.is_equal,
                                     0.0, 0, [[-1, 128]], 1),
              reads=["identf"], writes=["ident"])
        self.trimask = b.sbuf("trimask", [128, 128], BF16)
        p.add("vector", e_memset(self.trimask[:], 1.0), writes=["trimf"])
        p.add("gpsimd", e_affine_sel(self.trimask[:], self.trimask[:], ALU.is_ge,
                                     0.0, 0, [[1, 128]], -1),
              reads=["trimf"], writes=["trimask"])
        self.ones1 = b.sbuf("ones1", [1, 128], BF16)
        p.add("vector", e_memset(self.ones1[:], 1.0), writes=["ones1"])
        self.onesf = b.sbuf("onesf", [128, 1], F32)
        p.add("vector", e_memset(self.onesf[:], 1.0), writes=["onesf"])
        self.lnsc = b.sbuf("lnsc", [128, 8], F32)
        self.sqw = b.sbuf("sqw", [128, 512], F32)
        self.constv = b.sbuf("constv", [128, 16], F32)
        self._cb = {}

    def cbias(self, val):
        val = float(val)
        if val not in self._cb:
            i = len(self._cb)
            assert i < 16
            self._cb[val] = i
            self.p.add("vector", e_memset(self.constv[:, i:i + 1], val),
                       writes=["constv"])
        i = self._cb[val]
        return self.constv[:, i:i + 1]

    def ln_tile(self, xin, rtag, outs, D, eps, extra_scale=None):
        b, p = self, self.p
        s1 = self.lnsc[:, 0:1]
        s2 = self.lnsc[:, 1:2]
        s2p = self.lnsc[:, 2:3]
        m = self.lnsc[:, 3:4]
        r = self.lnsc[:, 4:5]
        p.add("vector", e_reduce(s1, xin, ALU.add), reads=[rtag],
              writes=["ln_s1"])
        nwin = D // 512
        for wdw in range(nwin):
            p.add("scalar", e_act(self.sqw[:], xin[:, wdw * 512:(wdw + 1) * 512],
                                  AF.Square, accum_out=s2p),
                  reads=[rtag], writes=["sqw", "ln_s2p"])
            if wdw == 0:
                p.add("vector", e_copy(s2, s2p), reads=["ln_s2p"], writes=["ln_s2"])
            else:
                p.add("vector", e_tt(s2, s2, s2p, ALU.add),
                      reads=["ln_s2p", "ln_s2"], writes=["ln_s2"])
        p.add("vector", e_ts(m, s1, 1.0 / D, ALU.mult), reads=["ln_s1"],
              writes=["ln_m"])
        p.add("vector", e_tt(r, m, m, ALU.mult), reads=["ln_m"], writes=["ln_r"])
        p.add("vector", e_stt(r, s2, 1.0 / D, r, ALU.mult, ALU.subtract),
              reads=["ln_s2", "ln_r"], writes=["ln_r"])
        p.add("scalar", e_act(r, r, AF.Sqrt, bias=self.cbias(eps)),
              reads=["ln_r", "constv"], writes=["ln_r"])
        p.add("vector", e_recip(r, r), reads=["ln_r"], writes=["ln_r"])
        p.add("vector", e_tt(m, m, r, ALU.mult), reads=["ln_m", "ln_r"],
              writes=["ln_m"])
        p.add("vector", e_ts(m, m, -1.0, ALU.mult), reads=["ln_m"],
              writes=["ln_m"])
        for (oap, wtag) in outs:
            p.add("vector", e_ts(oap, xin, r, ALU.mult, m, ALU.add),
                  reads=[rtag, "ln_m", "ln_r"], writes=[wtag])
        return m, r

    def trig_tables(self, sinb, cosb, sinsb, width, freq_mod, freq_scale,
                    freq_bias, sign_mask, tag):
        b, p = self, self.p
        tg = b.sbuf(tag + "_tg", [128, width], F32)
        tg2 = b.sbuf(tag + "_tg2", [128, width], F32)
        tgk = b.sbuf(tag + "_tgk", [128, width], I32)
        inv = b.sbuf(tag + "_inv", [128, 1], F32)
        sg = b.sbuf(tag + "_sg", [128, 1], F32)
        flg = b.sbuf(tag + "_flg", [128, 512], F32)
        p.add("gpsimd", e_iota(tgk[:, 0:1], [[0, 1]], 0, 1),
              writes=[tag + "k0"])
        p.add("vector", e_ts(tgk[:, 0:1], tgk[:, 0:1], freq_mod - 1,
                             ALU.bitwise_and),
              reads=[tag + "k0"], writes=[tag + "k0"])
        p.add("vector", e_copy(inv[:], tgk[:, 0:1]),
              reads=[tag + "k0"], writes=[tag + "inv"])
        fb = 0.0 if freq_bias == 0.0 else self.cbias(freq_bias)
        p.add("scalar", e_act(inv[:], inv[:], AF.Exp, scale=freq_scale,
                              bias=fb),
              reads=[tag + "inv"] + ([] if freq_bias == 0.0 else ["constv"]),
              writes=[tag + "inv"])
        p.add("gpsimd", e_iota(tgk[:, 1:2], [[0, 1]], 0, 1),
              writes=[tag + "k1"])
        p.add("vector", e_ts(tgk[:, 1:2], tgk[:, 1:2], sign_mask,
                             ALU.bitwise_and), reads=[tag + "k1"],
              writes=[tag + "k1"])
        p.add("vector", e_copy(sg[:], tgk[:, 1:2]), reads=[tag + "k1"],
              writes=[tag + "sg"])
        p.add("vector", e_ts(sg[:], sg[:], 2.0 / sign_mask, ALU.mult, -1.0,
                             ALU.add), reads=[tag + "sg"], writes=[tag + "sg"])
        p.add("gpsimd", e_iota(tg[:], [[1, width]], 0, 0), writes=[tag + "tg"])
        p.add("vector", e_ts(tg[:], tg[:], inv[:, 0:1], ALU.mult),
              reads=[tag + "tg", tag + "inv"], writes=[tag + "tg"])

        def reduce_and(out_ap, outtag, postsign):
            p.add("vector", e_ts(tg2[:], tg[:], 1.0 / (2 * math.pi), ALU.mult),
                  reads=[tag + "tg"], writes=[tag + "tg2"])
            p.add("vector", e_copy(tgk[:], tg2[:]), reads=[tag + "tg2"],
                  writes=[tag + "tgk"])
            p.add("vector", e_copy(tg2[:], tgk[:]), reads=[tag + "tgk"],
                  writes=[tag + "tg2"])
            p.add("vector", e_stt(tg2[:], tg2[:], -2 * math.pi, tg[:],
                                  ALU.mult, ALU.add),
                  reads=[tag + "tg2", tag + "tg"], writes=[tag + "tg2"])
            for (thr, delta) in ((math.pi, -2 * math.pi),
                                 (-math.pi, 2 * math.pi)):
                op = ALU.is_gt if delta < 0 else ALU.is_lt
                for wdw in range(width // 512):
                    sl = slice(wdw * 512, wdw * 512 + 512)
                    p.add("vector", e_ts(flg[:], tg2[:, sl], thr, op),
                          reads=[tag + "tg2"], writes=[tag + "flg"])
                    p.add("vector", e_stt(tg2[:, sl], flg[:], delta, tg2[:, sl],
                                          ALU.mult, ALU.add),
                          reads=[tag + "flg", tag + "tg2"],
                          writes=[tag + "tg2"])
            if postsign is None:
                p.add("scalar", e_act(out_ap, tg2[:], AF.Sin),
                      reads=[tag + "tg2"], writes=[outtag])
            else:
                p.add("scalar", e_act(tg2[:], tg2[:], AF.Sin),
                      reads=[tag + "tg2"], writes=[tag + "tg2"])
                p.add("vector", e_ts(out_ap, tg2[:], postsign, ALU.mult),
                      reads=[tag + "tg2", tag + "sg"], writes=[outtag])

        reduce_and(sinb[:], tag + "sin", None)
        p.add("vector", e_ts(tg2[:], tg[:], 1.0, ALU.bypass),
              reads=[tag + "tg"], writes=[tag + "tg2"])
        p.add("vector", e_ts(tg[:], tg[:], math.pi / 2, ALU.add),
              reads=[tag + "tg"], writes=[tag + "tg"])
        reduce_and(cosb[:], tag + "cos", None)
        p.add("vector", e_ts(tg[:], tg[:], -math.pi / 2, ALU.add),
              reads=[tag + "tg"], writes=[tag + "tg"])
        reduce_and(sinsb[:], tag + "sins", sg[:, 0:1])



def attention(b, lay, qT, qTtag, kT_dram, v_dram, sidx, vbt, aoT, aoTtag,
              nheads, hd, nkv, psp, scale, causal=True):
    p = b.p
    KW = nkv * hd
    nktile = 4
    nsl = SLOTS if causal else [4, 4]

    def slot_si(qb, j):
        if causal:
            return (0 if qb == 0 else SLOTS[0]) + j
        return 2 + j
    slot_sis = list(range(NSLOT)) if causal else [2, 3, 4, 5]
    VB = 65 if hd == 64 else 128
    kparts = (KW + 127) // 128
    kslw = (nkv * 512) if hd == 64 else (kparts * 512)
    ksl = b.sbuf(f"ksl{lay}", [128, NSLOT * kslw], BF16)
    vslw = nktile * nkv * VB
    vsl = b.sbuf(f"vsl{lay}", [128, NSLOT * vslw], BF16)
    asb = b.sbuf(f"asb{lay}", [128, 2 * 512], BF16)
    rawo = b.sbuf(f"rawo{lay}", [128, 16 * 512], BF16)
    zf32 = b.sbuf(f"zf32{lay}", [128, 2 * 512], F32)
    zsc = b.sbuf(f"zsc{lay}", [128, 64], F32)
    zscb = b.sbuf(f"zscb{lay}", [128, 64], BF16)
    zrecb = b.sbuf(f"zrecb{lay}", [1, 16 * 512], BF16)
    rzb = b.sbuf(f"rzb{lay}", [128, 512], BF16)
    vstA = b.sbuf(f"vstA{lay}", [128, 128 if lay else 256], BF16)
    ones128 = b.sbuf(f"ones128_{lay}", [128, 1], BF16)
    p.add("vector", e_memset(ones128[:], 1.0), writes=["ones128"])
    z_dram = b.nc.dram_tensor(f"z_dram{lay}", [1, 16 * 512], F32)
    zr_dram = b.nc.dram_tensor(f"zr_dram{lay}", [1, 16 * 512], BF16)

    p.add("sync", e_init_snaps(b, sidx, NSLOT, 512, "snapvals_s"),
          reads=["sidx"], writes=["snapvals_s"])
    p.add("sync", e_init_snaps(b, sidx, NSLOT, 512 * nkv * hd, "snapvals_sv"),
          reads=["sidx"], writes=["snapvals_sv"])
    p.add("scalar", e_init_snaps(b, sidx, NSLOT, 512 * nkv * hd, "snapvals_w"),
          reads=["sidx"], writes=["snapvals_w"])
    if hd == 64:
        ones_cols = vsl[:].rearrange("p (s g c) -> p s g c",
                                     s=NSLOT * nktile, g=nkv)[:, :, :, 64:65]
        p.add("vector", e_memset(ones_cols, 1.0), writes=["vsl"])

    for si in slot_sis:
        if True:
            if hd == 64:
                for g in range(nkv):
                    keng, kstore = (("gpsimd", "snapvals") if g < 3
                                    else ("sync", "snapvals_s"))
                    dst = ksl[:, si * kslw + g * 512:si * kslw + (g + 1) * 512]
                    p.add(keng,
                          e_ds2(b, dst,
                                (lambda g_: lambda off: kT_dram.ap()
                                 [g_ * 64:(g_ + 1) * 64, DS(off, 512)]
                                 .rearrange("(a p) t -> a p t", a=1)
                                 .broadcast_to([2, 64, 512]))(g),
                                si, store=kstore),
                          reads=["kT_dram", "sidx", kstore],
                          writes=["ksl"], dma=True)
            else:
                dst = ksl[:, si * kslw:si * kslw + 512]
                p.add("gpsimd",
                      e_ds2(b, dst,
                            lambda off: kT_dram.ap()[:, DS(off, 512)], si),
                      reads=["kT_dram", "sidx", "snaps"], writes=["ksl"],
                      dma=True)
            vw = nkv * hd
            vflat = v_dram.ap().rearrange("r c -> (r c)").rearrange(
                "(a e) -> a e", a=1)
            for kt in range(nktile):
                base = si * vslw + kt * nkv * VB
                veng = "sync" if kt < 2 else "scalar"
                vstore = "snapvals_sv" if kt < 2 else "snapvals_w"
                p.add(veng,
                      e_ds2(b, vstA[:],
                            (lambda kt_: lambda off: vflat
                             [0:1, kt_ * 128 * vw:][0:1, DS(off, 128 * vw)])(kt),
                            si, store=vstore),
                      reads=["v_dram", "sidx", vstore], writes=["vstA"],
                      dma=True)
                if hd == 64:
                    dst = vsl[:, base:base + nkv * VB]
                    dst = dst.rearrange("p (g c) -> p g c", g=nkv)[:, :, 0:hd]
                    p.add("sync", e_dma(dst, vstA[:].rearrange(
                        "p (g c) -> p g c", g=nkv)),
                          reads=["vstA"], writes=["vsl"], dma=True)
                else:
                    p.add("sync", e_dma(vsl[:, base:base + VB], vstA[:]),
                          reads=["vstA"], writes=["vsl"], dma=True)
            si += 1

    heads_per_tile = 128 // hd
    raw_used = 0
    for h in range(nheads):
        g = h // (nheads // nkv)
        dk = h // heads_per_tile
        rbase = 64 * (h % heads_per_tile) if hd == 64 else 0
        need_zmm = (hd == 128) or (rbase == 64)
        for qb in range(2):
            rslot = raw_used % 16
            if hd == 64 and rbase == 0:
                po = psp[0:65, 1024:1536]
                nout = 65
            else:
                po = psp[rbase:rbase + hd, 1024:1536]
                nout = hd
            first = True
            nj = nsl[qb]
            for j in range(nj):
                si = slot_si(qb, j)
                for kt in range(nktile):
                    bank = kt % 2
                    ps = psp[:, bank * 512:bank * 512 + 512]
                    if hd == 64:
                        lhsT = ksl[rbase:rbase + 64,
                                   si * kslw + g * 512 + kt * 128:
                                   si * kslw + g * 512 + kt * 128 + 128]
                    else:
                        lhsT = ksl[0:128,
                                   si * kslw + kt * 128:
                                   si * kslw + kt * 128 + 128]
                    rhs = qT[rbase:rbase + hd,
                             dk * TOWN + qb * 512:dk * TOWN + qb * 512 + 512]
                    p.add("tensor", e_mm(ps, lhsT, rhs),
                          reads=["ksl", qTtag], writes=[f"psb{bank}"])
                    at = asb[:, bank * 512:bank * 512 + 512]
                    vbap = vbt[:, si:si + 1]
                    if causal and j == 0 and kt > 0:
                        p.add("vector", e_memset(at[:, 0:kt * 128], 0.0),
                              writes=[f"asb{bank}"])
                        p.add("scalar",
                              e_act(at[:, kt * 128:512], ps[:, kt * 128:512],
                                    AF.Exp, bias=vbap, scale=scale),
                              reads=[f"psb{bank}", "vbt"],
                              writes=[f"asb{bank}"])
                    else:
                        p.add("scalar", e_act(at, ps, AF.Exp, bias=vbap,
                                              scale=scale),
                              reads=[f"psb{bank}", "vbt"], writes=[f"asb{bank}"])
                    if causal and j == 0:
                        sl = at[:, kt * 128:kt * 128 + 128]
                        p.add("vector", e_tt(sl, sl, b.trimask[:], ALU.mult),
                              reads=[f"asb{bank}", "trimask"],
                              writes=[f"asb{bank}"])
                    last = (j == nj - 1) and (kt == nktile - 1)
                    if hd == 64:
                        vbl = si * vslw + kt * nkv * VB + g * VB
                        width = 65 if nout == 65 else 64
                        vl = vsl[:, vbl:vbl + width]
                    else:
                        vl = vsl[:, si * vslw + kt * VB:si * vslw + kt * VB + VB]
                    p.add("tensor", e_mm(po, vl, at, start=first, stop=last),
                          reads=["vsl", f"asb{bank}"], writes=["psb2"])
                    if need_zmm:
                        pz = psp[0:1, 1536:2048]
                        p.add("tensor", e_mm(pz, ones128[:], at,
                                             start=first, stop=last),
                              reads=["ones128", f"asb{bank}"], writes=["psb3"])
                    first = False
            zr = zf32[:, (raw_used % 2) * 512:(raw_used % 2) * 512 + 512]
            ztag = f"zf32{raw_used % 2}"
            if hd == 64:
                p.add("vector", e_copy(rawo[rbase:rbase + 64,
                                            rslot * 512:rslot * 512 + 512],
                                       po[0 if rbase == 64 else 0:64]
                                       if False else
                                       (po[0:64] if rbase == 0 else po)),
                      reads=["psb2"], writes=["rawo"])
            else:
                p.add("vector", e_copy(rawo[:, rslot * 512:rslot * 512 + 512],
                                       po),
                      reads=["psb2"], writes=["rawo"])
            if need_zmm:
                p.add("vector", e_copy(zr[0:1, :], psp[0:1, 1536:2048]),
                      reads=["psb3"], writes=[ztag])
                p.add("sync", e_dma(z_dram.ap()[0:1,
                                                rslot * 512:rslot * 512 + 512],
                                    zr[0:1, :]),
                      reads=[ztag], writes=["z_dram"], dma=True)
            else:
                p.add("vector", e_copy(zr[64:65, :], po[64:65, :]),
                      reads=["psb2"], writes=[ztag])
                p.add("sync", e_dma(z_dram.ap()[0:1,
                                                rslot * 512:rslot * 512 + 512],
                                    zr[64:65, :]),
                      reads=[ztag], writes=["z_dram"], dma=True)
            raw_used += 1
            if raw_used % 16 == 0:
                _norm_group(b, raw_used, zsc, zscb, zrecb, rzb, rawo, aoT,
                            aoTtag, z_dram, zr_dram, hd, psp)
    assert raw_used % 16 == 0


def _norm_group(b, raw_used, zsc, zscb, zrecb, rzb, rawo, aoT,
                aoTtag, z_dram, zr_dram, hd, psp):
    p = b.p
    grp = raw_used // 16 - 1
    p.add("sync", e_dma(zsc[:],
                        z_dram.ap().rearrange("o (p f) -> o p f", p=128)),
          reads=["z_dram"], writes=["zsc"], dma=True)
    p.add("vector", e_recip(zsc[:], zsc[:]), reads=["zsc"], writes=["zsc"])
    p.add("vector", e_copy(zscb[:], zsc[:]), reads=["zsc"], writes=["zscb"])
    p.add("sync", e_dma(zr_dram.ap().rearrange("o (p f) -> o p f", p=128),
                        zscb[:]),
          reads=["zscb"], writes=["zr_dram"], dma=True)
    p.add("sync", e_dma(zrecb[0:1, :], zr_dram.ap()),
          reads=["zr_dram"], writes=["zrecb"], dma=True)
    heads_per_tile = 128 // hd
    for t in range(16):
        hq = grp * 16 + t
        h = hq // 2
        qb = hq % 2
        dk = h // heads_per_tile
        rbase = 64 * (h % heads_per_tile) if hd == 64 else 0
        nr = hd
        pz = psp[rbase:rbase + nr, 1024:1536]
        p.add("tensor", e_mm(pz, b.ones1[0:1, 0:nr],
                             zrecb[0:1, t * 512:t * 512 + 512]),
              reads=["ones1", "zrecb"], writes=["psb2"])
        p.add("scalar", e_copy(rzb[rbase:rbase + nr, :], pz), reads=["psb2"],
              writes=["rzb"])
        dst = aoT[rbase:rbase + nr,
                  dk * TOWN + qb * 512:dk * TOWN + qb * 512 + 512]
        p.add("vector", e_tt(dst, rawo[rbase:rbase + nr,
                                       t * 512:t * 512 + 512],
                             rzb[rbase:rbase + nr, :], ALU.mult),
              reads=["rawo", "rzb"], writes=[aoTtag])



NW = 8192 * 2048


def build_l0(dbg=()):
    nc = bass.Bass("TRN2", target_bir_lowering=False)
    b = Builder(nc)
    p = b.p

    x_own = nc.dram_tensor("x_own", [TOWN, DM], F32, kind="ExternalInput").ap()
    x_full = nc.dram_tensor("x_full", [SEQ, DM], F32, kind="ExternalInput").ap()
    kv_a_w = nc.dram_tensor("kv_a_w", [DM, 512], F32, kind="ExternalInput").ap()
    kv_b_w = nc.dram_tensor("kv_b_w", [512, 512], F32, kind="ExternalInput").ap()
    o_w = nc.dram_tensor("o_w", [DM, DM], F32, kind="ExternalInput").ap()
    wfcT = nc.dram_tensor("wfcT", [DM, 8192], F32, kind="ExternalInput").ap()
    wprojT = nc.dram_tensor("wprojT", [8192, DM], F32, kind="ExternalInput").ap()
    slotidx = nc.dram_tensor("slotidx", [1, 8], I32, kind="ExternalInput").ap()
    validb = nc.dram_tensor("validb", [1, 8], F32, kind="ExternalInput").ap()
    x2_out = nc.dram_tensor("x2_out", [TOWN, DM], F32, kind="ExternalOutput").ap()
    dbg_outs = {}

    x_dram = nc.dram_tensor("x_dram", [TOWN, DM], F32)
    kT_dram = nc.dram_tensor("kT_dram", [256, SEQ], BF16)
    v_dram = nc.dram_tensor("v_dram", [SEQ, 256], BF16)
    trig_dram = nc.dram_tensor("trig_dram", [384, SEQ], BF16)
    hg_dram = nc.dram_tensor("hg_dram", [TOWN, 8192], F32)
    tern_dram = nc.dram_tensor("tern_dram", [8192, DM], BF16)

    b.constants()
    xt = [b.sbuf(f"xt{i}", [128, DM], F32) for i in range(2)]
    wstg = b.sbuf("wstg", [128, 2 * 512], F32)
    wbf = b.sbuf("wbf", [128, 2 * 512], BF16)
    p1s = b.sbuf("p1s", [128, 512], F32)
    xr = b.sbuf("xr", [128, 2 * 512], F32)
    sidx = b.sbuf("sidx", [1, 8], I32)
    vbt = b.sbuf("vbt", [128, 8], F32)
    p.add("sync", e_dma(sidx[:], slotidx), writes=["sidx"], dma=True)
    p.add("sync", e_dma(vbt[:], validb.rearrange("(a o) e -> a o e", a=1)
                        .broadcast_to([1, 128, 8])),
          writes=["vbt"], dma=True)
    psp = b.psum("psp", [128, 4096], F32)
    pst = psp[:, 3904:3968].bitcast(BF16)

    b.push()
    sinf = b.sbuf("sinf", [128, SEQ], BF16)
    cosf = b.sbuf("cosf", [128, SEQ], BF16)
    sinSf = b.sbuf("sinSf", [128, SEQ], BF16)
    sino = b.sbuf("sino", [128, TOWN], BF16)
    coso = b.sbuf("coso", [128, TOWN], BF16)
    sinSo = b.sbuf("sinSo", [128, TOWN], BF16)
    b.push()
    b.trig_tables(sinf, cosf, sinSf, SEQ, 32, -LN2_E / 32.0, 0.0, 32, "tg0")
    b.pop()
    p.add("sync", e_dma(trig_dram.ap()[0:128, :], sinf[:]), reads=["tg0sin"],
          writes=["trig_dram"], dma=True)
    p.add("sync", e_dma(trig_dram.ap()[128:256, :], cosf[:]), reads=["tg0cos"],
          writes=["trig_dram"], dma=True)
    p.add("sync", e_dma(trig_dram.ap()[256:384, :], sinSf[:]), reads=["tg0sins"],
          writes=["trig_dram"], dma=True)
    p.add("gpsimd", e_init_snaps(b, sidx, NSLOT, 512, "snapvals"),
          reads=["sidx"], writes=["snaps"])
    p.add("scalar", e_init_snaps(b, sidx, SLOTS[0] + 1, 512, "snapvals_a"),
          reads=["sidx"], writes=["snaps_a"])
    for qb in range(2):
        si = 0 if qb == 0 else SLOTS[0]
        for (dst, dtag, roff) in ((sino, "sino", 0), (coso, "coso", 128),
                                  (sinSo, "sinSo", 256)):
            p.add("scalar",
                  e_ds2(b, dst[:, qb * 512:qb * 512 + 512],
                        (lambda ro: lambda off: trig_dram.ap()
                         [ro:ro + 128, DS(off, 512)])(roff), si,
                        store="snapvals_a"),
                  reads=["trig_dram", "sidx", "snaps_a"], writes=[dtag],
                  dma=True)

    qT = b.sbuf("qT", [128, 16 * TOWN], BF16)

    b.push()
    yot = b.sbuf("yot", [128, DM], BF16)
    yoT = b.sbuf("yoT", [128, 16 * TOWN], BF16)
    yoTs = b.sbuf("yoTs", [128, 16 * TOWN], BF16)
    for i in range(8):
        tile = xt[i % 2]
        tg = f"xt{i % 2}"
        p.add("sync", e_dma(tile[:], x_own[i * 128:(i + 1) * 128, :]),
              writes=[tg], dma=True)
        p.add("sync", e_dma(x_dram.ap()[i * 128:(i + 1) * 128, :], tile[:]),
              reads=[tg], writes=["x_dram"], dma=True)
        b.ln_tile(tile[:], tg, [(yot[:], "yot")], DM, FP32_EPS_LN)
        for dk in range(16):
            p.add("tensor", e_transpose(
                pst, yot[:, dk * 128:dk * 128 + 128],
                b.ident[:]), reads=["yot", "ident"], writes=["psb7"])
            p.add("scalar", e_copy(
                yoT[:, dk * TOWN + i * 128:dk * TOWN + i * 128 + 128], pst),
                reads=["psb7"], writes=["yoT"])
    for (d0, s0) in ((0, 32), (32, 0), (64, 96), (96, 64)):
        p.add("sync", e_dma(yoTs[d0:d0 + 32, :], yoT[s0:s0 + 32, :]),
              reads=["yoT"], writes=["yoTs"], dma=True)
    cosb = coso[:].rearrange("p (a t) -> p a t", a=1).broadcast_to([128, 16, TOWN])
    sinb = sinSo[:].rearrange("p (a t) -> p a t", a=1).broadcast_to([128, 16, TOWN])
    p.add("vector", e_tt(qT[:], yoT[:], cosb, ALU.mult),
          reads=["yoT", "coso"], writes=["qT"])
    p.add("vector", e_tt(yoTs[:], yoTs[:], sinb, ALU.mult),
          reads=["yoTs", "sinSo"], writes=["yoTs"])
    p.add("vector", e_tt(qT[:], qT[:], yoTs[:], ALU.add),
          reads=["qT", "yoTs"], writes=["qT"])
    b.pop()

    b.push()
    yfTb = b.sbuf("yfTb", [128, 16 * 512], BF16)
    yf = b.sbuf("yf", [128, DM], BF16)
    kva = b.sbuf("kva", [128, 16 * 512], BF16)
    kvb = b.sbuf("kvb", [128, 4 * 512], BF16)
    kvcTb = b.sbuf("kvcTb", [128, 4 * 512], BF16)
    kvsbb = b.sbuf("kvsbb", [128, 4 * 512], BF16)
    kswb = b.sbuf("kswb", [64, 4 * 512], BF16)
    vstg = b.sbuf("vstg", [128, 256], BF16)
    idhi = b.sbuf("idhi", [128, 64], BF16)
    p.add("vector", e_memset(idhi[:], 1.0), writes=["idhif"])
    p.add("gpsimd", e_affine_sel(idhi[:], idhi[:], ALU.is_equal, 0.0, -64,
                                 [[-1, 64]], 1),
          reads=["idhif"], writes=["idhi"])
    for dk in range(16):
        w = wstg[:, (dk % 2) * 512:(dk % 2) * 512 + 512]
        p.add("sync", e_dma(w, kv_a_w[dk * 128:(dk + 1) * 128, :]),
              writes=[f"wstg{dk % 2}"], dma=True)
        p.add("vector", e_copy(kva[:, dk * 512:(dk + 1) * 512], w),
              reads=[f"wstg{dk % 2}"], writes=["kva"])
    for kc in range(4):
        w = wstg[:, (kc % 2) * 512:(kc % 2) * 512 + 512]
        p.add("sync", e_dma(w, kv_b_w[kc * 128:(kc + 1) * 128, :]),
              writes=[f"wstg{kc % 2}"], dma=True)
        p.add("vector", e_copy(kvb[:, kc * 512:(kc + 1) * 512], w),
              reads=[f"wstg{kc % 2}"], writes=["kvb"])
    for tb in range(4):
        for ii in range(4):
            i = tb * 4 + ii
            tile = xt[i % 2]
            tg = f"xt{i % 2}"
            p.add("sync", e_dma(tile[:], x_full[i * 128:(i + 1) * 128, :]),
                  writes=[tg], dma=True)
            b.ln_tile(tile[:], tg, [(yf[:], "yf")], DM, FP32_EPS_LN)
            for dk in range(16):
                p.add("tensor", e_transpose(
                    pst, yf[:, dk * 128:dk * 128 + 128], b.ident[:]),
                    reads=["yf", "ident"], writes=["psb7"])
                p.add("scalar", e_copy(
                    yfTb[:, dk * 512 + ii * 128:dk * 512 + ii * 128 + 128],
                    pst), reads=["psb7"], writes=["yfTb"])
        for oc in range(4):
            ps = psp[:, 0:512]
            chain = []
            for dk in range(16):
                lhsT = kva[:, dk * 512 + oc * 128:dk * 512 + oc * 128 + 128]
                rhs = yfTb[:, dk * 512:(dk + 1) * 512]
                chain.append((ps, lhsT, rhs, dk == 0, dk == 15))
            p.add("tensor", e_mm_chain(chain), reads=["kva", "yfTb"],
                  writes=["psb0"])
            p.add("scalar", e_copy(kvcTb[:, oc * 512:(oc + 1) * 512], ps),
                  reads=["psb0"], writes=["kvcTb"])
        for gc in range(4):
            ps = psp[:, 0:512]
            ps1 = psp[:, 512:1024]
            ck, cv, cs1, cs2 = [], [], [], []
            for kc in range(4):
                rhs = kvcTb[:, kc * 512:(kc + 1) * 512]
                base = kc * 512 + gc * 128
                ck.append((ps[0:64, :], kvb[:, base:base + 64], rhs,
                           kc == 0, kc == 3))
                cv.append((ps[64:128, :], kvb[:, base + 64:base + 128], rhs,
                           kc == 0, kc == 3))
                cs1.append((ps1[0:32, :], kvb[:, base + 32:base + 64], rhs,
                            kc == 0, kc == 3))
                cs2.append((ps1[32:64, :], kvb[:, base:base + 32], rhs,
                            kc == 0, kc == 3))
            p.add("tensor", e_mm_chain(ck + cv), reads=["kvb", "kvcTb"],
                  writes=["psb0"])
            p.add("tensor", e_mm_chain(cs1 + cs2), reads=["kvb", "kvcTb"],
                  writes=["psb1"])
            sl = slice(gc * 512, (gc + 1) * 512)
            p.add("scalar", e_copy(kvsbb[:, sl], ps), reads=["psb0"],
                  writes=["kvsbb"])
            p.add("scalar", e_copy(kswb[0:64, sl], ps1[0:64, :]),
                  reads=["psb1"], writes=["kswb"])
        cosb = cosf[0:64, tb * 512:(tb + 1) * 512].rearrange(
            "p (a t) -> p a t", a=1).broadcast_to([64, 4, 512])
        sinb = sinSf[0:64, tb * 512:(tb + 1) * 512].rearrange(
            "p (a t) -> p a t", a=1).broadcast_to([64, 4, 512])
        kvv = kvsbb[0:64, :].rearrange("p (a t) -> p a t", a=4)
        kwv = kswb[0:64, :].rearrange("p (a t) -> p a t", a=4)
        p.add("vector", e_tt(kvv, kvv, cosb, ALU.mult),
              reads=["kvsbb", "tg0cos"], writes=["kvsbb"])
        p.add("vector", e_tt(kwv, kwv, sinb, ALU.mult),
              reads=["kswb", "tg0sins"], writes=["kswb"])
        p.add("vector", e_tt(kvsbb[0:64, :], kvsbb[0:64, :], kswb[0:64, :],
                             ALU.add),
              reads=["kvsbb", "kswb"], writes=["kvsbb"])
        for gc in range(4):
            p.add("sync", e_dma(kT_dram.ap()[gc * 64:(gc + 1) * 64,
                                             tb * 512:(tb + 1) * 512],
                                kvsbb[0:64, gc * 512:(gc + 1) * 512]),
                  reads=["kvsbb"], writes=["kT_dram"], dma=True)
        for tc in range(4):
            for gc in range(4):
                p.add("tensor", e_transpose(
                    pst[:, 0:64],
                    kvsbb[64:128, gc * 512 + tc * 128:gc * 512 + tc * 128 + 128],
                    idhi[64:128, :]),
                    reads=["kvsbb", "idhi"], writes=["psb7"])
                p.add("scalar", e_copy(vstg[:, gc * 64:(gc + 1) * 64],
                                       pst[:, 0:64]),
                      reads=["psb7"], writes=["vstg"])
            row = tb * 512 + tc * 128
            p.add("sync", e_dma(v_dram.ap()[row:row + 128, :], vstg[:]),
                  reads=["vstg"], writes=["v_dram"], dma=True)
    b.pop()

    if "kT" in dbg:
        dbg_outs["kT_dbg"] = (kT_dram, [256, SEQ], BF16, "kT_dram")
        dbg_outs["v_dbg"] = (v_dram, [SEQ, 256], BF16, "v_dram")
    if "qT" in dbg:
        qT_dram = nc.dram_tensor("qT_dram", [128, 16 * TOWN], BF16)
        p.add("sync", e_dma(qT_dram.ap(), qT[:]), reads=["qT"],
              writes=["qT_dram"], dma=True)
        dbg_outs["qT_dbg"] = (qT_dram, [128, 16 * TOWN], BF16, "qT_dram")

    aoT = b.sbuf("aoT", [128, 16 * TOWN], BF16)
    b.push()
    attention(b, 0, qT, "qT", kT_dram, v_dram, sidx, vbt, aoT, "aoT",
              L0_NH, 64, L0_NKV, psp, 0.125)
    b.pop()

    for ob in range(4):
        for dk in range(16):
            w = wstg[:, (dk % 2) * 512:(dk % 2) * 512 + 512]
            wb = wbf[:, (dk % 2) * 512:(dk % 2) * 512 + 512]
            p.add("sync", e_dma(w, o_w[dk * 128:(dk + 1) * 128,
                                       ob * 512:(ob + 1) * 512]),
                  writes=[f"wstg{dk % 2}"], dma=True)
            p.add("vector", e_copy(wb, w), reads=[f"wstg{dk % 2}"],
                  writes=[f"wbf{dk % 2}"])
            for t8 in range(8):
                ps = psp[:, t8 * 512:t8 * 512 + 512]
                lhsT = aoT[:, dk * TOWN + t8 * 128:dk * TOWN + t8 * 128 + 128]
                p.add("tensor", e_mm(ps, lhsT, wb, start=(dk == 0),
                                     stop=(dk == 15)),
                      reads=["aoT", f"wbf{dk % 2}"], writes=[f"psb{t8 % 8}"])
        for t8 in range(8):
            x0 = xr[:, (t8 % 2) * 512:(t8 % 2) * 512 + 512]
            ps = psp[:, t8 * 512:t8 * 512 + 512]
            p.add("sync", e_dma(x0, x_dram.ap()[t8 * 128:(t8 + 1) * 128,
                                                ob * 512:(ob + 1) * 512]),
                  reads=["x_dram"], writes=[f"xr{t8 % 2}"], dma=True)
            p.add("vector", e_tt(x0, x0, ps, ALU.add),
                  reads=[f"xr{t8 % 2}", f"psb{t8}"], writes=[f"xr{t8 % 2}"])
            p.add("sync", e_dma(x_dram.ap()[t8 * 128:(t8 + 1) * 128,
                                            ob * 512:(ob + 1) * 512], x0),
                  reads=[f"xr{t8 % 2}"], writes=["x_dram"], dma=True)
    if "aoT" in dbg:
        aoT_dram = nc.dram_tensor("aoT_dram", [128, 16 * TOWN], BF16)
        p.add("sync", e_dma(aoT_dram.ap(), aoT[:]), reads=["aoT"],
              writes=["aoT_dram"], dma=True)
        dbg_outs["aoT_dbg"] = (aoT_dram, [128, 16 * TOWN], BF16, "aoT_dram")
    b.pop()

    if "xattn" in dbg:
        dbg_outs["xattn_dbg"] = (x_dram, [TOWN, DM], F32, "x_dram")

    ffn_l0(b, x_dram, wfcT, wprojT, hg_dram, tern_dram, x2_out, psp, pst,
           xt, wstg, wbf, p1s, xr)

    p.add("vector", e_nop(), reads=["x2_out"])

    if dbg_outs:
        p.barrier()
        for name, (dram, shape, dt, rtag) in dbg_outs.items():
            out = nc.dram_tensor(name, shape, dt, kind="ExternalOutput").ap()
            p.add("gpsimd", e_dma(out, dram.ap()), reads=[rtag],
                  writes=["dbgw"], dma=True)
        p.add("vector", e_nop(), reads=["dbgw"])

    return nc, b


def ffn_l0(b, x_dram, wfcT, wprojT, hg_dram, tern_dram, x2_out, psp, pst,
           xt, wstg, wbf, p1s, xr):
    nc, p = b.nc, b.p
    thr = b.sbuf("thr", [128, 4], F32)
    ngam = b.sbuf("ngam", [128, 2], F32)
    gbc = b.sbuf("gbc", [128, 2], F32)
    absc = b.sbuf("absc", [128, 4], F32)

    b.push()
    gacc = b.sbuf("gacc", [128, 64], F32)
    gsc = b.sbuf("gsc", [1, 2], F32)
    onesr = b.sbuf("onesr", [1, 128], F32)
    p.add("vector", e_memset(onesr[:], 1.0), writes=["onesr"])
    wg = b.sbuf("wg", [128, 2 * 4096], F32)
    wabs = b.sbuf("wabs", [128, 4096], F32)
    for wt, (src_ap, nrows, wid) in enumerate(((wfcT, 16, 8192),
                                               (wprojT, 64, 2048))):
        nsplit = wid // 4096 if wid >= 4096 else 1
        tw = wid // nsplit
        nt = 0
        for rr in range(nrows):
            for cc in range(nsplit):
                w = wg[:, (nt % 2) * 4096:(nt % 2) * 4096 + tw]
                p.add("sync", e_dma(w, src_ap[rr * 128:(rr + 1) * 128,
                                              cc * tw:(cc + 1) * tw]),
                      writes=[f"wg{nt % 2}"], dma=True)
                acc_col = gacc[:, (nt % 64):(nt % 64) + 1]
                p.add("scalar", e_act(wabs[:, 0:tw], w, AF.Abs,
                                      accum_out=acc_col),
                      reads=[f"wg{nt % 2}"],
                      writes=[f"gacc{nt % 64}", "wabs"])
                nt += 1
                if nt % 64 == 0 or nt == nrows * nsplit:
                    ncols = 64 if nt % 64 == 0 else nt % 64
                    dst = gbc[:, wt:wt + 1]
                    tmp = absc[:, 0:1]
                    p.add("vector", e_reduce(tmp, gacc[:, 0:ncols], ALU.add),
                          reads=[f"gacc{i}" for i in range(ncols)],
                          writes=["gtmp"])
                    if nt <= 64:
                        p.add("vector", e_copy(dst, tmp),
                              reads=["gtmp"], writes=[f"gbc{wt}"])
                    else:
                        p.add("vector", e_tt(dst, dst, tmp, ALU.add),
                              reads=["gtmp", f"gbc{wt}"], writes=[f"gbc{wt}"])
        p.add("tensor", e_mm(psp[0:1, 0:1], gbc[:, wt:wt + 1], b.onesf[:]),
              reads=[f"gbc{wt}", "onesf"], writes=["psb0"])
        p.add("vector", e_copy(gsc[0:1, wt:wt + 1], psp[0:1, 0:1]),
              reads=["psb0"], writes=["gsc"])
        p.add("tensor", e_mm(psp[0:128, 0:1], onesr[0:1, :],
                             gsc[0:1, wt:wt + 1]),
              reads=["onesr", "gsc"], writes=["psb0"])
        p.add("vector", e_copy(gbc[:, wt:wt + 1], psp[0:128, 0:1]),
              reads=["psb0"], writes=[f"gbc{wt}"])
        p.add("vector", e_ts(thr[:, 2 * wt:2 * wt + 1], gbc[:, wt:wt + 1],
                             0.5 / NW, ALU.mult, 0.5 * FP32_EPS_BL, ALU.add),
              reads=[f"gbc{wt}"], writes=["thr"])
        p.add("vector", e_ts(thr[:, 2 * wt + 1:2 * wt + 2],
                             thr[:, 2 * wt:2 * wt + 1], -1.0, ALU.mult),
              reads=["thr"], writes=["thr"])
        p.add("vector", e_ts(ngam[:, wt:wt + 1], gbc[:, wt:wt + 1],
                             -1.0 / NW, ALU.mult),
              reads=[f"gbc{wt}"], writes=["ngam"])
    b.pop()

    zqT = b.sbuf("zqT", [128, 16 * TOWN], BF16)
    b.push()
    zt = b.sbuf("zt", [128, DM], F32)
    znt = b.sbuf("znt", [128, DM], F32)
    zqb = b.sbuf("zqb", [128, DM], BF16)
    for i in range(8):
        tile = xt[i % 2]
        tg = f"xt{i % 2}"
        p.add("sync", e_dma(tile[:], x_dram.ap()[i * 128:(i + 1) * 128, :]),
              reads=["x_dram"], writes=[tg], dma=True)
        b.ln_tile(tile[:], tg, [(zt[:], "zt")], DM, FP32_EPS_LN)
        b.ln_tile(zt[:], "zt", [(znt[:], "znt")], DM, FP32_EPS_BL)
        absmax_quant(b, znt[:], "znt", zqb[:], "zqb", absc)
        for dk in range(16):
            p.add("tensor", e_transpose(
                pst, zqb[:, dk * 128:dk * 128 + 128], b.ident[:]),
                reads=["zqb", "ident"], writes=["psb7"])
            p.add("scalar", e_copy(
                zqT[:, dk * TOWN + i * 128:dk * TOWN + i * 128 + 128], pst),
                reads=["psb7"], writes=["zqT"])
    b.pop()

    b.push()
    hgst = b.sbuf("hgst", [128, 2 * 512], F32)
    for ob in range(16):
        for dk in range(16):
            w = wstg[:, (dk % 2) * 512:(dk % 2) * 512 + 512]
            wb = wbf[:, (dk % 2) * 512:(dk % 2) * 512 + 512]
            p.add("sync", e_dma(w, wfcT[dk * 128:(dk + 1) * 128,
                                        ob * 512:(ob + 1) * 512]),
                  writes=[f"wstg{dk % 2}"], dma=True)
            ternarize(b, w, f"wstg{dk % 2}", wb, f"wbf{dk % 2}", thr, 0, p1s)
            for t8 in range(8):
                ps = psp[:, t8 * 512:t8 * 512 + 512]
                lhsT = zqT[:, dk * TOWN + t8 * 128:dk * TOWN + t8 * 128 + 128]
                p.add("tensor", e_mm(ps, lhsT, wb, start=(dk == 0),
                                     stop=(dk == 15)),
                      reads=["zqT", f"wbf{dk % 2}"], writes=[f"psb{t8}"])
        for t8 in range(8):
            hh = hgst[:, (t8 % 2) * 512:(t8 % 2) * 512 + 512]
            p.add("scalar", e_act(hh, psp[:, t8 * 512:t8 * 512 + 512],
                                  AF.Gelu, scale=ngam[:, 0:1]),
                  reads=[f"psb{t8}", "ngam"], writes=[f"hgst{t8 % 2}"])
            p.add("sync", e_dma(hg_dram.ap()[t8 * 128:(t8 + 1) * 128,
                                             ob * 512:(ob + 1) * 512], hh),
                  reads=[f"hgst{t8 % 2}"], writes=["hg_dram"], dma=True)
    for hk in range(64):
        for oc in range(4):
            w = wstg[:, (oc % 2) * 512:(oc % 2) * 512 + 512]
            wb = wbf[:, (oc % 2) * 512:(oc % 2) * 512 + 512]
            p.add("sync", e_dma(w, wprojT[hk * 128:(hk + 1) * 128,
                                          oc * 512:(oc + 1) * 512]),
                  writes=[f"wstg{oc % 2}"], dma=True)
            ternarize(b, w, f"wstg{oc % 2}", wb, f"wbf{oc % 2}", thr, 1, p1s)
            p.add("sync", e_dma(tern_dram.ap()[hk * 128:(hk + 1) * 128,
                                               oc * 512:(oc + 1) * 512], wb),
                  reads=[f"wbf{oc % 2}"], writes=["tern_dram"], dma=True)
    b.pop()

    b.push()
    hgl = b.sbuf("hgl", [128, 8192], F32)
    hb = b.sbuf("hb", [128, 8192], BF16)
    hT = b.sbuf("hT", [128, 4 * 64 * 128], BF16)
    tp = b.sbuf("tp", [128, 2 * 512], BF16)
    for sup in range(2):
        for t4 in range(4):
            t = sup * 4 + t4
            p.add("sync", e_dma(hgl[:], hg_dram.ap()[t * 128:(t + 1) * 128, :]),
                  reads=["hg_dram"], writes=["hgl"], dma=True)
            b.ln_tile(hgl[:], "hgl", [(hgl[:], "hgl")], 8192, FP32_EPS_BL)
            absmax_quant(b, hgl[:], "hgl", hb[:], "hb", absc)
            for hk in range(64):
                p.add("tensor", e_transpose(
                    pst, hb[:, hk * 128:hk * 128 + 128], b.ident[:]),
                    reads=["hb", "ident"], writes=["psb7"])
                p.add("scalar", e_copy(
                    hT[:, t4 * 8192 + hk * 128:t4 * 8192 + hk * 128 + 128],
                    pst), reads=["psb7"], writes=["hT"])
        for ob in range(4):
            for hk in range(64):
                tpt = tp[:, (hk % 2) * 512:(hk % 2) * 512 + 512]
                p.add("sync", e_dma(tpt, tern_dram.ap()
                                    [hk * 128:(hk + 1) * 128,
                                     ob * 512:(ob + 1) * 512]),
                      reads=["tern_dram"], writes=[f"tp{hk % 2}"], dma=True)
                for t4 in range(4):
                    ps = psp[:, t4 * 512:t4 * 512 + 512]
                    lhsT = hT[:, t4 * 8192 + hk * 128:t4 * 8192 + hk * 128 + 128]
                    p.add("tensor", e_mm(ps, lhsT, tpt, start=(hk == 0),
                                         stop=(hk == 63)),
                          reads=["hT", f"tp{hk % 2}"], writes=[f"psb{t4}"])
            for t4 in range(4):
                t = sup * 4 + t4
                x0 = xr[:, (t4 % 2) * 512:(t4 % 2) * 512 + 512]
                p.add("sync", e_dma(x0, x_dram.ap()[t * 128:(t + 1) * 128,
                                                    ob * 512:(ob + 1) * 512]),
                      reads=["x_dram"], writes=[f"xr{t4 % 2}"], dma=True)
                p.add("vector", e_stt(x0, psp[:, t4 * 512:t4 * 512 + 512],
                                      ngam[:, 1:2], x0, ALU.mult, ALU.add),
                      reads=[f"psb{t4}", "ngam", f"xr{t4 % 2}"],
                      writes=[f"xr{t4 % 2}"])
                p.add("sync", e_dma(x2_out[t * 128:(t + 1) * 128,
                                           ob * 512:(ob + 1) * 512], x0),
                      reads=[f"xr{t4 % 2}"], writes=["x2_out"], dma=True)
    b.pop()


def absmax_quant(b, src_ap, stag, dst_ap, dtag, absc):
    p = b.p
    gm = absc[:, 0:1]
    rg = absc[:, 1:2]
    c1 = absc[:, 2:3]
    c2 = absc[:, 3:4]
    p.add("vector", e_reduce(gm, src_ap, ALU.max, absval=True),
          reads=[stag], writes=["am_gm"])
    p.add("vector", e_recip(rg, gm), reads=["am_gm"], writes=["am_rg"])
    p.add("vector", e_ts(c1, rg, 128.0, ALU.mult), reads=["am_rg"],
          writes=["am_c1"])
    p.add("vector", e_ts(c2, gm, 1.0 / 128.0, ALU.mult), reads=["am_gm"],
          writes=["am_c2"])
    p.add("vector", e_ts(dst_ap, src_ap, c1, ALU.mult, CLIP_HI, ALU.min),
          reads=[stag, "am_c1"], writes=[dtag])
    p.add("vector", e_ts(dst_ap, dst_ap, -CLIP_HI, ALU.max, c2, ALU.mult),
          reads=[dtag, "am_c2"], writes=[dtag])


def ternarize(b, w_ap, wtag, out_ap, otag, thr, wt, p1s):
    p = b.p
    p.add("vector", e_ts(p1s[:], w_ap, thr[:, 2 * wt:2 * wt + 1], ALU.is_gt),
          reads=[wtag, "thr"], writes=["p1s"])
    p.add("vector", e_stt(out_ap, w_ap, thr[:, 2 * wt + 1:2 * wt + 2],
                          p1s[:], ALU.is_lt, ALU.subtract),
          reads=[wtag, "thr", "p1s"], writes=[otag])


def finalize(nc, b):
    hs = []
    sems = {}
    for e in Plan.ENGINES:
        h = nc.semaphore(f"S_{e}")
        sems[e] = h.__enter__()
        hs.append(h)
    with nc.Block() as block:
        b.p.emit_all(block, sems)
    for h in reversed(hs):
        h.__exit__(None, None, None)
    b.close()
    return nc


def stage_l0_inputs(inputs, core):
    x0 = inputs["L0_attn_hidden_states"]
    qbs, seq = chunk_assign(core)
    xs = np.ascontiguousarray(x0[seq])
    x_own = np.ascontiguousarray(
        np.concatenate([xs[c * 512:(c + 1) * 512] for c in qbs], axis=0))
    sidx, vb = slot_meta(core)
    return {
        "x_own": x_own,
        "x_full": xs,
        "kv_a_w": inputs["L0_attn_kv_a_w"],
        "kv_b_w": inputs["L0_attn_kv_b_w"],
        "o_w": inputs["L0_attn_o_w"],
        "wfcT": inputs["_wfcT"],
        "wprojT": inputs["_wprojT"],
        "slotidx": sidx.reshape(1, 8),
        "validb": vb.reshape(1, 8),
    }


def host_prep(inputs):
    out = dict(inputs)
    out["_wfcT"] = np.ascontiguousarray(inputs["L0_ffn_w_fc"].T)
    out["_wprojT"] = np.ascontiguousarray(inputs["L0_ffn_w_proj"].T)

    def bytes_of(w):
        return np.ascontiguousarray(
            w.view(np.uint8).reshape(w.shape[0], w.shape[1], 4)[:, :, 0].T)

    out["_gbytes"] = bytes_of(inputs["L1_ffn_gate_W_packed"])
    out["_ubytes"] = bytes_of(inputs["L1_ffn_up_W_packed"])
    out["_dbytes"] = bytes_of(inputs["L1_ffn_down_W_packed"])
    perm_n = np.concatenate([s * 1024 + np.r_[0:1024:2, 1:1024:2]
                             for s in range(16)])
    out["_sg_sig"] = np.ascontiguousarray(
        inputs["L1_ffn_gate_scales"].T[:, perm_n])
    out["_su_sig"] = np.ascontiguousarray(
        inputs["L1_ffn_up_scales"].T[:, perm_n])
    perm_h = np.concatenate([bk * 512 + np.r_[0:512:2, 1:512:2]
                             for bk in range(4)])
    out["_sd_sig"] = np.ascontiguousarray(
        inputs["L1_ffn_down_scales"].T[:, perm_h])
    return out


def assemble_stream(parts):
    full = np.empty((4, SEQ, DM), dtype=parts[0].dtype)
    for core in range(NCORE):
        qbs, seq = chunk_assign(core)
        for qi, c in enumerate(qbs):
            full[seq, c * 512:(c + 1) * 512] = parts[core][qi * 512:(qi + 1) * 512]
    return full



def build_l1(dbg=()):
    nc = bass.Bass("TRN2", target_bir_lowering=False)
    b = Builder(nc)
    p = b.p

    x_own = nc.dram_tensor("x_own", [TOWN, DM], F32, kind="ExternalInput").ap()
    x_full = nc.dram_tensor("x_full", [SEQ, DM], F32, kind="ExternalInput").ap()
    w_attn = nc.dram_tensor("w_attn", [DM, 2304], F32, kind="ExternalInput").ap()
    w_out = nc.dram_tensor("w_out", [DM, DM], F32, kind="ExternalInput").ap()
    gbytes = nc.dram_tensor("gbytes", [1024, 16384], U8, kind="ExternalInput").ap()
    ubytes = nc.dram_tensor("ubytes", [1024, 16384], U8, kind="ExternalInput").ap()
    dbytes = nc.dram_tensor("dbytes", [8192, DM], U8, kind="ExternalInput").ap()
    sg_sig = nc.dram_tensor("sg_sig", [64, 16384], F32, kind="ExternalInput").ap()
    su_sig = nc.dram_tensor("su_sig", [64, 16384], F32, kind="ExternalInput").ap()
    sd_sig = nc.dram_tensor("sd_sig", [512, DM], F32, kind="ExternalInput").ap()
    slotidx = nc.dram_tensor("slotidx", [1, 8], I32, kind="ExternalInput").ap()
    validb = nc.dram_tensor("validb", [1, 8], F32, kind="ExternalInput").ap()
    x3_out = nc.dram_tensor("x3_out", [TOWN, DM], F32, kind="ExternalOutput").ap()
    dbg_outs = {}

    x_dram = nc.dram_tensor("x_dram", [TOWN, DM], F32)
    kT_dram = nc.dram_tensor("kT_dram", [128, SEQ], BF16)
    v_dram = nc.dram_tensor("v_dram", [SEQ, 128], BF16)
    trig_dram = nc.dram_tensor("trig_dram", [384, SEQ], BF16)
    interT_dram = nc.dram_tensor("interT_dram", [16384, TOWN], BF16)
    sgb_dram = nc.dram_tensor("sgb_dram", [64, 16384], BF16)
    sub_dram = nc.dram_tensor("sub_dram", [64, 16384], BF16)
    sdb_dram = nc.dram_tensor("sdb_dram", [512, DM], BF16)

    b.constants()
    xt = [b.sbuf(f"xt{i}", [128, DM], F32) for i in range(2)]
    wstg = b.sbuf("wstg", [128, 2 * 512], F32)
    wbf = b.sbuf("wbf", [128, 2 * 512], BF16)
    xr = b.sbuf("xr", [128, 2 * 512], F32)
    sidx = b.sbuf("sidx", [1, 8], I32)
    vbt = b.sbuf("vbt", [128, 8], F32)
    p.add("sync", e_dma(sidx[:], slotidx), writes=["sidx"], dma=True)
    p.add("sync", e_dma(vbt[:], validb.rearrange("(a o) e -> a o e", a=1)
                        .broadcast_to([1, 128, 8])),
          writes=["vbt"], dma=True)
    psp = b.psum("psp", [128, 4096], F32)
    pst = psp[:, 3904:3968].bitcast(BF16)

    b.push()
    scnv = b.sbuf("scnv", [128, 4096], F32)
    scnb = b.sbuf("scnb", [128, 4096], BF16)
    for (srcap, dstd, rows, cols, tg) in ((sg_sig, sgb_dram, 64, 16384, "sgb"),
                                          (su_sig, sub_dram, 64, 16384, "sub"),
                                          (sd_sig, sdb_dram, 512, DM, "sdb")):
        nrt = rows // 128 if rows >= 128 else 1
        prt = min(rows, 128)
        nct = (cols + 4095) // 4096
        for rr in range(nrt):
            for cc in range(nct):
                w = min(4096, cols - cc * 4096)
                p.add("sync", e_dma(scnv[0:prt, 0:w],
                                    srcap[rr * 128:rr * 128 + prt,
                                          cc * 4096:cc * 4096 + w]),
                      writes=["scnv"], dma=True)
                p.add("vector", e_copy(scnb[0:prt, 0:w], scnv[0:prt, 0:w]),
                      reads=["scnv"], writes=["scnb"])
                p.add("sync", e_dma(dstd.ap()[rr * 128:rr * 128 + prt,
                                              cc * 4096:cc * 4096 + w],
                                    scnb[0:prt, 0:w]),
                      reads=["scnb"], writes=[tg], dma=True)
    b.pop()

    b.push()
    sinf = b.sbuf("sinf", [128, SEQ], BF16)
    cosf = b.sbuf("cosf", [128, SEQ], BF16)
    sinSf = b.sbuf("sinSf", [128, SEQ], BF16)
    sino = b.sbuf("sino", [128, TOWN], BF16)
    coso = b.sbuf("coso", [128, TOWN], BF16)
    sinSo = b.sbuf("sinSo", [128, TOWN], BF16)
    b.push()
    b.trig_tables(sinf, cosf, sinSf, SEQ, 64, -4.0 * LN2_E / 128.0,
                  -LN2_E / 128.0, 64, "tg1")
    b.pop()
    p.add("sync", e_dma(trig_dram.ap()[0:128, :], sinf[:]), reads=["tg1sin"],
          writes=["trig_dram"], dma=True)
    p.add("sync", e_dma(trig_dram.ap()[128:256, :], cosf[:]), reads=["tg1cos"],
          writes=["trig_dram"], dma=True)
    p.add("sync", e_dma(trig_dram.ap()[256:384, :], sinSf[:]), reads=["tg1sins"],
          writes=["trig_dram"], dma=True)
    p.add("gpsimd", e_init_snaps(b, sidx, NSLOT, 512, "snapvals"),
          reads=["sidx"], writes=["snaps"])
    p.add("sync", e_init_snaps(b, sidx, NSLOT, 512, "snapvals_s"),
          reads=["sidx"], writes=["snapvals_s"])
    p.add("sync", e_init_snaps(b, sidx, NSLOT, 512 * 128, "snapvals_sv"),
          reads=["sidx"], writes=["snapvals_sv"])
    p.add("scalar", e_init_snaps(b, sidx, NSLOT, 512 * 128, "snapvals_w"),
          reads=["sidx"], writes=["snapvals_w"])
    p.add("scalar", e_init_snaps(b, sidx, 2, 512, "snapvals_a"),
          reads=["sidx"], writes=["snaps_a"])
    for qb in range(2):
        si = qb
        for (dst, dtag, roff) in ((sino, "sino", 0), (coso, "coso", 128),
                                  (sinSo, "sinSo", 256)):
            p.add("scalar",
                  e_ds2(b, dst[:, qb * 512:qb * 512 + 512],
                        (lambda ro: lambda off: trig_dram.ap()
                         [ro:ro + 128, DS(off, 512)])(roff), si,
                        store="snapvals_a"),
                  reads=["trig_dram", "sidx", "snaps_a"], writes=[dtag],
                  dma=True)

    qT = b.sbuf("qT", [128, 16 * TOWN], BF16)

    b.push()
    yot = b.sbuf("yot", [128, DM], BF16)
    yoT = b.sbuf("yoT", [128, 16 * TOWN], BF16)
    qTs = b.sbuf("qTs", [128, 16 * TOWN], BF16)
    wat = b.sbuf("wat", [128, 16 * 128], BF16)
    for i in range(8):
        tile = xt[i % 2]
        tg = f"xt{i % 2}"
        p.add("sync", e_dma(tile[:], x_own[i * 128:(i + 1) * 128, :]),
              writes=[tg], dma=True)
        p.add("sync", e_dma(x_dram.ap()[i * 128:(i + 1) * 128, :], tile[:]),
              reads=[tg], writes=["x_dram"], dma=True)
        b.ln_tile(tile[:], tg, [(yot[:], "yot")], DM, FP32_EPS_LN)
        for dk in range(16):
            p.add("tensor", e_transpose(
                pst, yot[:, dk * 128:dk * 128 + 128], b.ident[:]),
                reads=["yot", "ident"], writes=["psb7"])
            p.add("scalar", e_copy(
                yoT[:, dk * TOWN + i * 128:dk * TOWN + i * 128 + 128], pst),
                reads=["psb7"], writes=["yoT"])
    for oc in range(16):
        for dk in range(16):
            w = wstg[:, (dk % 2) * 512:(dk % 2) * 512 + 128]
            p.add("sync", e_dma(w, w_attn[dk * 128:(dk + 1) * 128,
                                          oc * 128:(oc + 1) * 128]),
                  writes=[f"wstg{dk % 2}"], dma=True)
            p.add("vector", e_copy(wat[:, dk * 128:(dk + 1) * 128], w),
                  reads=[f"wstg{dk % 2}"], writes=["wat"])
        for tb in range(2):
            ps = psp[:, 0:512]
            chain = []
            for dk in range(16):
                lhsT = wat[:, dk * 128:(dk + 1) * 128]
                rhs = yoT[:, dk * TOWN + tb * 512:dk * TOWN + tb * 512 + 512]
                chain.append((ps, lhsT, rhs, dk == 0, dk == 15))
            p.add("tensor", e_mm_chain(chain), reads=["wat", "yoT"],
                  writes=["psb0"])
            p.add("scalar", e_copy(qT[:, oc * TOWN + tb * 512:
                                      oc * TOWN + tb * 512 + 512], ps),
                  reads=["psb0"], writes=["qTr"])
    for (d0, s0) in ((0, 64), (64, 0)):
        p.add("sync", e_dma(qTs[d0:d0 + 64, :], qT[s0:s0 + 64, :]),
              reads=["qTr"], writes=["qTs"], dma=True)
    cosb = coso[:].rearrange("p (a t) -> p a t", a=1).broadcast_to([128, 16, TOWN])
    sinb = sinSo[:].rearrange("p (a t) -> p a t", a=1).broadcast_to([128, 16, TOWN])
    p.add("vector", e_tt(qT[:], qT[:], cosb, ALU.mult),
          reads=["qTr", "coso"], writes=["qT", "qTr"])
    p.add("vector", e_tt(qTs[:], qTs[:], sinb, ALU.mult),
          reads=["qTs", "sinSo"], writes=["qTs"])
    p.add("vector", e_tt(qT[:], qT[:], qTs[:], ALU.add),
          reads=["qT", "qTs"], writes=["qT"])
    b.pop()

    b.push()
    yfTb = b.sbuf("yfTb", [128, 16 * 512], BF16)
    yf = b.sbuf("yf", [128, DM], BF16)
    wkv = b.sbuf("wkv", [128, 16 * 256], BF16)
    ktb = b.sbuf("ktb", [128, 512], BF16)
    ksb = b.sbuf("ksb", [128, 512], BF16)
    vsb = b.sbuf("vsb", [128, 512], BF16)
    vstg = b.sbuf("vstg", [128, 128], BF16)
    for dk in range(16):
        w = wstg[:, (dk % 2) * 512:(dk % 2) * 512 + 256]
        p.add("sync", e_dma(w, w_attn[dk * 128:(dk + 1) * 128, 2048:2304]),
              writes=[f"wstg{dk % 2}"], dma=True)
        p.add("vector", e_copy(wkv[:, dk * 256:(dk + 1) * 256], w),
              reads=[f"wstg{dk % 2}"], writes=["wkv"])
    for tb in range(4):
        for ii in range(4):
            i = tb * 4 + ii
            tile = xt[i % 2]
            tg = f"xt{i % 2}"
            p.add("sync", e_dma(tile[:], x_full[i * 128:(i + 1) * 128, :]),
                  writes=[tg], dma=True)
            b.ln_tile(tile[:], tg, [(yf[:], "yf")], DM, FP32_EPS_LN)
            for dk in range(16):
                p.add("tensor", e_transpose(
                    pst, yf[:, dk * 128:dk * 128 + 128], b.ident[:]),
                    reads=["yf", "ident"], writes=["psb7"])
                p.add("scalar", e_copy(
                    yfTb[:, dk * 512 + ii * 128:dk * 512 + ii * 128 + 128],
                    pst), reads=["psb7"], writes=["yfTb"])
        ps = psp[:, 0:512]
        ps1 = psp[:, 512:1024]
        ps2 = psp[:, 1024:1536]
        ck, cs1, cs2, cv = [], [], [], []
        for dk in range(16):
            rhs = yfTb[:, dk * 512:(dk + 1) * 512]
            wk = wkv[:, dk * 256:dk * 256 + 128]
            wv = wkv[:, dk * 256 + 128:dk * 256 + 256]
            ck.append((ps, wk, rhs, dk == 0, dk == 15))
            cs1.append((ps1[0:64, :], wk[:, 64:128], rhs, dk == 0, dk == 15))
            cs2.append((ps1[64:128, :], wk[:, 0:64], rhs, dk == 0, dk == 15))
            cv.append((ps2, wv, rhs, dk == 0, dk == 15))
        p.add("tensor", e_mm_chain(ck), reads=["wkv", "yfTb"], writes=["psb0"])
        p.add("tensor", e_mm_chain(cs1 + cs2), reads=["wkv", "yfTb"],
              writes=["psb1"])
        p.add("tensor", e_mm_chain(cv), reads=["wkv", "yfTb"], writes=["psb2"])
        p.add("scalar", e_copy(ktb[:], ps), reads=["psb0"], writes=["ktb"])
        p.add("scalar", e_copy(ksb[:], ps1), reads=["psb1"], writes=["ksb"])
        p.add("scalar", e_copy(vsb[:], ps2), reads=["psb2"], writes=["vsb"])
        csl = cosf[:, tb * 512:(tb + 1) * 512]
        ssl = sinSf[:, tb * 512:(tb + 1) * 512]
        p.add("vector", e_tt(ktb[:], ktb[:], csl, ALU.mult),
              reads=["ktb", "tg1cos"], writes=["ktb"])
        p.add("vector", e_tt(ksb[:], ksb[:], ssl, ALU.mult),
              reads=["ksb", "tg1sins"], writes=["ksb"])
        p.add("vector", e_tt(ktb[:], ktb[:], ksb[:], ALU.add),
              reads=["ktb", "ksb"], writes=["ktb"])
        p.add("sync", e_dma(kT_dram.ap()[:, tb * 512:(tb + 1) * 512], ktb[:]),
              reads=["ktb"], writes=["kT_dram"], dma=True)
        for tc in range(4):
            p.add("tensor", e_transpose(
                pst, vsb[:, tc * 128:(tc + 1) * 128], b.ident[:]),
                reads=["vsb", "ident"], writes=["psb7"])
            p.add("scalar", e_copy(vstg[:], pst), reads=["psb7"],
                  writes=["vstg"])
            row = tb * 512 + tc * 128
            p.add("sync", e_dma(v_dram.ap()[row:row + 128, :], vstg[:]),
                  reads=["vstg"], writes=["v_dram"], dma=True)
    b.pop()

    if "kT" in dbg:
        dbg_outs["kT_dbg"] = (kT_dram, [128, SEQ], BF16, "kT_dram")
        dbg_outs["v_dbg"] = (v_dram, [SEQ, 128], BF16, "v_dram")
    if "qT" in dbg:
        qT_dram = nc.dram_tensor("qT_dram", [128, 16 * TOWN], BF16)
        p.add("sync", e_dma(qT_dram.ap(), qT[:]), reads=["qT"],
              writes=["qT_dram"], dma=True)
        dbg_outs["qT_dbg"] = (qT_dram, [128, 16 * TOWN], BF16, "qT_dram")

    aoT = b.sbuf("aoT", [128, 16 * TOWN], BF16)
    b.push()
    attention(b, 1, qT, "qT", kT_dram, v_dram, sidx, vbt, aoT, "aoT",
              L1_NH, 128, 1, psp, float(128 ** -0.5), causal=False)
    b.pop()

    for ob in range(4):
        for dk in range(16):
            w = wstg[:, (dk % 2) * 512:(dk % 2) * 512 + 512]
            wb = wbf[:, (dk % 2) * 512:(dk % 2) * 512 + 512]
            p.add("sync", e_dma(w, w_out[dk * 128:(dk + 1) * 128,
                                         ob * 512:(ob + 1) * 512]),
                  writes=[f"wstg{dk % 2}"], dma=True)
            p.add("vector", e_copy(wb, w), reads=[f"wstg{dk % 2}"],
                  writes=[f"wbf{dk % 2}"])
            for t8 in range(8):
                ps = psp[:, t8 * 512:t8 * 512 + 512]
                lhsT = aoT[:, dk * TOWN + t8 * 128:dk * TOWN + t8 * 128 + 128]
                p.add("tensor", e_mm(ps, lhsT, wb, start=(dk == 0),
                                     stop=(dk == 15)),
                      reads=["aoT", f"wbf{dk % 2}"], writes=[f"psb{t8}"])
        for t8 in range(8):
            x0 = xr[:, (t8 % 2) * 512:(t8 % 2) * 512 + 512]
            ps = psp[:, t8 * 512:t8 * 512 + 512]
            p.add("sync", e_dma(x0, x_dram.ap()[t8 * 128:(t8 + 1) * 128,
                                                ob * 512:(ob + 1) * 512]),
                  reads=["x_dram"], writes=[f"xr{t8 % 2}"], dma=True)
            p.add("vector", e_tt(x0, x0, ps, ALU.add),
                  reads=[f"xr{t8 % 2}", f"psb{t8}"], writes=[f"xr{t8 % 2}"])
            p.add("sync", e_dma(x_dram.ap()[t8 * 128:(t8 + 1) * 128,
                                            ob * 512:(ob + 1) * 512], x0),
                  reads=[f"xr{t8 % 2}"], writes=["x_dram"], dma=True)
    b.pop()

    if "xattn" in dbg:
        dbg_outs["xattn_dbg"] = (x_dram, [TOWN, DM], F32, "x_dram")

    ffn_l1(b, x_dram, gbytes, ubytes, dbytes, sgb_dram, sub_dram, sdb_dram,
           interT_dram, x3_out, psp, pst, xt, xr, dbg, dbg_outs, nc)

    p.add("vector", e_nop(), reads=["x3_out"])

    if dbg_outs:
        p.barrier()
        for name, (dram, shape, dt_, rtag) in dbg_outs.items():
            out = nc.dram_tensor(name, shape, dt_, kind="ExternalOutput").ap()
            p.add("gpsimd", e_dma(out, dram.ap()), reads=[rtag],
                  writes=["dbgw"], dma=True)
        p.add("vector", e_nop(), reads=["dbgw"])

    return nc, b


def nv_decode(b, Pap, ptag, out_ap, otag, srep_ap, sreptag, sc, spv=None,
              sp8v=None, ushift=None):
    p = b.p
    W4 = Pap.shape[1]
    W2 = 2 * W4
    m0 = sc["m0"][:, 0:W4]
    m1 = sc["m1"][:, 0:W4]
    if ushift is None:
        p.add("vector", e_ts(m0, Pap, spv, ALU.logical_shift_right,
                             0x000F000F, ALU.bitwise_and),
              reads=[ptag, "spv"], writes=["dm0"])
        p.add("vector", e_ts(m1, Pap, sp8v, ALU.logical_shift_right,
                             0x000F000F, ALU.bitwise_and),
              reads=[ptag, "spv"], writes=["dm1"])
    else:
        p.add("vector", e_ts(m0, Pap, ushift, ALU.logical_shift_right,
                             0x000F000F, ALU.bitwise_and),
              reads=[ptag], writes=["dm0"])
        p.add("vector", e_ts(m1, Pap, ushift + 8, ALU.logical_shift_right,
                             0x000F000F, ALU.bitwise_and),
              reads=[ptag], writes=["dm1"])
    a1 = sc["a1"][:, 0:W2].bitcast(I16)
    a2 = sc["a2"][:, 0:W2].bitcast(I16)
    c1n = sc["c1n"][:, 0:W2].bitcast(I16)
    s16 = sc["s16"][:, 0:W2].bitcast(I16)
    ext = [ptag, sreptag]
    for half, (X, xtag) in enumerate(((m0, "dm0"), (m1, "dm1"))):
        X16 = X.bitcast(I16)
        dst = out_ap[:, half * W2:(half + 1) * W2]
        dst16 = dst.bitcast(I16)
        srep = srep_ap[:, half * W2:(half + 1) * W2]
        p.add("vector", e_ts(a1, X16, 7, ALU.bitwise_and, 6,
                             ALU.logical_shift_left),
              reads=[xtag] + ext, writes=["da1"])
        p.add("vector", e_ts(a2, a1, 16128, ALU.add),
              reads=["da1"] + ext, writes=["da2"])
        p.add("vector", e_ts(c1n, a1, 251, ALU.mult, -16128, ALU.add),
              reads=["da1"] + ext, writes=["dc1n"])
        p.add("vector", e_stt(dst16, c1n, 0.0, a2, ALU.min, ALU.add),
              reads=["dc1n", "da2"] + ext, writes=[otag])
        p.add("vector", e_ts(s16, X16, 8, ALU.bitwise_and, 12,
                             ALU.logical_shift_left),
              reads=[xtag] + ext, writes=["ds16"])
        p.add("vector", e_tt(dst16, dst16, s16, ALU.bitwise_or),
              reads=[otag, "ds16"] + ext, writes=[otag])
        p.add("vector", e_tt(dst, dst, srep, ALU.mult),
              reads=[otag, sreptag] + ext, writes=[otag])


def ffn_l1(b, x_dram, gbytes, ubytes, dbytes, sgb_dram, sub_dram, sdb_dram,
           interT_dram, x3_out, psp, pst, xt, xr, dbg, dbg_outs, nc):
    p = b.p
    GSC = 128.0 ** 0

    zqT = b.sbuf("zqT", [128, 16 * TOWN], BF16)
    b.push()
    zqb = b.sbuf("zqb", [128, DM], BF16)
    for i in range(8):
        tile = xt[i % 2]
        tg = f"xt{i % 2}"
        p.add("sync", e_dma(tile[:], x_dram.ap()[i * 128:(i + 1) * 128, :]),
              reads=["x_dram"], writes=[tg], dma=True)
        b.ln_tile(tile[:], tg, [(zqb[:], "zqb")], DM, FP32_EPS_LN)
        for dk in range(16):
            p.add("tensor", e_transpose(
                pst, zqb[:, dk * 128:dk * 128 + 128], b.ident[:]),
                reads=["zqb", "ident"], writes=["psb7"])
            p.add("scalar", e_copy(
                zqT[:, dk * TOWN + i * 128:dk * TOWN + i * 128 + 128], pst),
                reads=["psb7"], writes=["zqT"])
    b.pop()
    if "zqT" in dbg:
        zqT_dram = nc.dram_tensor("zqT_dram", [128, 16 * TOWN], BF16)
        p.add("sync", e_dma(zqT_dram.ap(), zqT[:]), reads=["zqT"],
              writes=["zqT_dram"], dma=True)
        dbg_outs["zqT_dbg"] = (zqT_dram, [128, 16 * TOWN], BF16, "zqT_dram")

    b.push()
    spv = b.sbuf("spv", [128, 2], I32)
    p.add("gpsimd", e_iota(spv[:, 0:1], [[0, 1]], 0, 1), writes=["spv0"])
    p.add("vector", e_ts(spv[:, 0:1], spv[:, 0:1], 1, ALU.bitwise_and, 2,
                         ALU.logical_shift_left),
          reads=["spv0"], writes=["spv"])
    p.add("vector", e_ts(spv[:, 1:2], spv[:, 0:1], 8, ALU.add),
          reads=["spv"], writes=["spv"])
    dec_g = b.sbuf("dec_g", [128, 16 * 1024], BF16)
    dec_u = b.sbuf("dec_u", [128, 16 * 1024], BF16)
    byt = b.sbuf("byt", [128, 2 * 1024], U8)
    srg = b.sbuf("srg", [128, 2 * 1024], BF16)
    sc = {
        "m0": b.sbuf("dsc_m0", [128, 256], I32),
        "m1": b.sbuf("dsc_m1", [128, 256], I32),
        "a1": b.sbuf("dsc_a1", [128, 512], BF16),
        "a2": b.sbuf("dsc_a2", [128, 512], BF16),
        "c1n": b.sbuf("dsc_c1n", [128, 512], BF16),
        "s16": b.sbuf("dsc_s16", [128, 512], BF16),
    }
    gsb = b.sbuf("gsb", [128, 512], BF16)
    usb = b.sbuf("usb", [128, 512], BF16)
    isb = b.sbuf("isb", [128, 2 * 512], BF16)
    import os
    _sg = os.environ.get("SER_GATE", "")
    p.ser_on = _sg == "1"
    for sup in range(16):
        if _sg == "dec":
            p.ser_on = True
        for mi, (byts, sdram, dec, dtag) in enumerate(
                ((gbytes, sgb_dram, dec_g, "dec_g"),
                 (ubytes, sub_dram, dec_u, "dec_u"))):
            for dk in range(16):
                bt = byt[:, (dk % 2) * 1024:(dk % 2) * 1024 + 1024]
                btag = f"byt{dk % 2}"
                src = byts[dk * 64:(dk + 1) * 64, sup * 1024:(sup + 1) * 1024]
                src = src.rearrange("(j a) n -> j a n", a=1)
                src = src.broadcast_to([64, 2, 1024])
                if _sg == "ld":
                    p.ser_on = True
                p.add("sync", e_dma(bt, src), writes=[btag], dma=True)
                sr = srg[:, (dk % 2) * 1024:(dk % 2) * 1024 + 1024]
                srtag = f"srg{dk % 2}"
                ssrc = sdram.ap()[dk * 4:(dk + 1) * 4,
                                  sup * 1024:(sup + 1) * 1024]
                ssrc = ssrc.rearrange("(r a) n -> r a n", a=1)
                ssrc = ssrc.broadcast_to([4, 32, 1024])
                p.add("sync", e_dma(sr, ssrc), writes=[srtag], dma=True)
                if _sg == "ld":
                    p.ser_on = False
                if _sg == "vec":
                    p.ser_on = True
                nv_decode(b, bt.bitcast(I32), btag,
                          dec[:, dk * 1024:(dk + 1) * 1024], dtag,
                          sr, srtag, sc, spv=spv[:, 0:1], sp8v=spv[:, 1:2])
                if _sg == "vec":
                    p.ser_on = False
        if _sg == "dec":
            p.ser_on = False
        if _sg == "mm":
            p.ser_on = True
        for tb in range(2):
            for nc8 in range(8):
                psg = psp[:, 0:512]
                psu = psp[:, 512:1024]
                cg, cu = [], []
                for dk in range(16):
                    rhs = zqT[:, dk * TOWN + tb * 512:dk * TOWN + tb * 512 + 512]
                    cg.append((psg, dec_g[:, dk * 1024 + nc8 * 128:
                                          dk * 1024 + nc8 * 128 + 128],
                               rhs, dk == 0, dk == 15))
                    cu.append((psu, dec_u[:, dk * 1024 + nc8 * 128:
                                          dk * 1024 + nc8 * 128 + 128],
                               rhs, dk == 0, dk == 15))
                p.add("tensor", e_mm_chain(cg), reads=["dec_g", "zqT"],
                      writes=["psb0"])
                p.add("tensor", e_mm_chain(cu), reads=["dec_u", "zqT"],
                      writes=["psb1"])
                p.add("scalar", e_act(gsb[:], psg, AF.Gelu_apprx_tanh),
                      reads=["psb0"], writes=["gsb"])
                p.add("scalar", e_copy(usb[:], psu), reads=["psb1"],
                      writes=["usb"])
                ii = isb[:, (nc8 % 2) * 512:(nc8 % 2) * 512 + 512]
                p.add("vector", e_tt(ii, gsb[:], usb[:], ALU.mult),
                      reads=["gsb", "usb"], writes=[f"isb{nc8 % 2}"])
                row = sup * 1024 + nc8 * 128
                p.add("sync", e_dma(interT_dram.ap()
                                    [row:row + 128, tb * 512:(tb + 1) * 512],
                                    ii),
                      reads=[f"isb{nc8 % 2}"], writes=["interT"], dma=True)
        if _sg == "mm":
            p.ser_on = False
    p.ser_on = False
    b.pop()

    if "inter" in dbg:
        dbg_outs["inter_dbg"] = (interT_dram, [16384, TOWN], BF16, "interT")

    b.push()
    dbyt = b.sbuf("dbyt", [128, 2 * 512], U8)
    srd = b.sbuf("srd", [128, 2 * 512], BF16)
    dec_d = b.sbuf("dec_d", [128, 2 * 512], BF16)
    itc = b.sbuf("itc", [128, 2 * TOWN], BF16)
    sc2 = {
        "m0": b.sbuf("d2_m0", [128, 128], I32),
        "m1": b.sbuf("d2_m1", [128, 128], I32),
        "a1": b.sbuf("d2_a1", [128, 256], BF16),
        "a2": b.sbuf("d2_a2", [128, 256], BF16),
        "c1n": b.sbuf("d2_c1n", [128, 256], BF16),
        "s16": b.sbuf("d2_s16", [128, 256], BF16),
    }
    for hb in range(4):
        for ncc in range(128):
            s, cc = ncc // 8, ncc % 8
            rot = ncc % 2
            bt = dbyt[:, rot * 512:rot * 512 + 512]
            btag = f"dbyt{rot}"
            row0 = 512 * s + 128 * (cc % 4)
            p.add("sync", e_dma(bt, dbytes[row0:row0 + 128,
                                           hb * 512:(hb + 1) * 512]),
                  writes=[btag], dma=True)
            sr = srd[:, rot * 512:rot * 512 + 512]
            srtag = f"srd{rot}"
            srow = 32 * s + 8 * (cc % 4)
            ssrc = sdb_dram.ap()[srow:srow + 8, hb * 512:(hb + 1) * 512]
            ssrc = ssrc.rearrange("(r a) n -> r a n", a=1)
            ssrc = ssrc.broadcast_to([8, 16, 512])
            p.add("sync", e_dma(sr, ssrc), writes=[srtag], dma=True)
            dd = dec_d[:, rot * 512:rot * 512 + 512]
            ddtag = f"dec_d{rot}"
            nv_decode(b, bt.bitcast(I32), btag, dd, ddtag, sr, srtag, sc2,
                      ushift=(4 if cc >= 4 else 0))
            itt = itc[:, rot * TOWN:(rot + 1) * TOWN]
            ittag = f"itc{rot}"
            p.add("sync", e_dma(itt, interT_dram.ap()
                                [ncc * 128:(ncc + 1) * 128, :]),
                  reads=["interT"], writes=[ittag], dma=True)
            for t8 in range(8):
                ps = psp[:, t8 * 512:t8 * 512 + 512]
                lhsT = itt[:, t8 * 128:(t8 + 1) * 128]
                p.add("tensor", e_mm(ps, lhsT, dd, start=(ncc == 0),
                                     stop=(ncc == 127)),
                      reads=[ittag, ddtag], writes=[f"psb{t8}"])
        for t8 in range(8):
            x0 = xr[:, (t8 % 2) * 512:(t8 % 2) * 512 + 512]
            xtag = f"xr{t8 % 2}"
            p.add("sync", e_dma(x0, x_dram.ap()[t8 * 128:(t8 + 1) * 128,
                                                hb * 512:(hb + 1) * 512]),
                  reads=["x_dram"], writes=[xtag], dma=True)
            ps = psp[:, t8 * 512:t8 * 512 + 512]
            xv = x0.rearrange("p (c two) -> p two c", two=2)
            p.add("vector", e_tt(xv[:, 0, :], xv[:, 0, :], ps[:, 0:256],
                                 ALU.add),
                  reads=[xtag, f"psb{t8}"], writes=[xtag])
            p.add("vector", e_tt(xv[:, 1, :], xv[:, 1, :], ps[:, 256:512],
                                 ALU.add),
                  reads=[xtag, f"psb{t8}"], writes=[xtag])
            p.add("sync", e_dma(x3_out[t8 * 128:(t8 + 1) * 128,
                                       hb * 512:(hb + 1) * 512], x0),
                  reads=[xtag], writes=["x3_out"], dma=True)
    b.pop()


def stage_l1_inputs(inputs, x2_full, core):
    qbs, seq = chunk_assign(core)
    xs = np.ascontiguousarray(x2_full[seq])
    x_own = np.ascontiguousarray(
        np.concatenate([xs[c * 512:(c + 1) * 512] for c in qbs], axis=0))
    sidx, vb = slot_meta_full(core)
    return {
        "x_own": x_own,
        "x_full": xs,
        "w_attn": inputs["L1_attn_w_attn"],
        "w_out": inputs["L1_attn_w_out"],
        "gbytes": inputs["_gbytes"],
        "ubytes": inputs["_ubytes"],
        "dbytes": inputs["_dbytes"],
        "sg_sig": inputs["_sg_sig"],
        "su_sig": inputs["_su_sig"],
        "sd_sig": inputs["_sd_sig"],
        "slotidx": sidx.reshape(1, 8),
        "validb": vb.reshape(1, 8),
    }


def kernel(**inputs):
    from concourse.bass_utils import run_bass_kernel_spmd
    pre = host_prep(inputs)
    nc0, b0 = build_l0()
    nc0 = finalize(nc0, b0)
    res0 = run_bass_kernel_spmd(
        nc0, [stage_l0_inputs(pre, c) for c in range(NCORE)],
        core_ids=list(range(NCORE)))
    x2_full = assemble_stream(
        [np.asarray(res0.results[c]["x2_out"]) for c in range(NCORE)])
    nc1, b1 = build_l1()
    nc1 = finalize(nc1, b1)
    res1 = run_bass_kernel_spmd(
        nc1, [stage_l1_inputs(pre, x2_full, c) for c in range(NCORE)],
        core_ids=list(range(NCORE)))
    return assemble_stream(
        [np.asarray(res1.results[c]["x3_out"]) for c in range(NCORE)])
